# version 07
# baseline (speedup 1.0000x reference)
import numpy as np
from contextlib import ExitStack

import concourse.bass as bass
import concourse.mybir as mybir
from concourse.bass_utils import run_bass_kernel_spmd

F32 = mybir.dt.float32
BF16 = mybir.dt.bfloat16
AF = mybir.ActivationFunctionType
ALU = mybir.AluOpType
AX = mybir.AxisListType

NCORE = 8
NB = 4
S = 2048
DM = 1024
NCH = DM // 128
NST = S // 128
H = 16
DK = 64
IFF = 2816
EPS_LN = 1e-5
EPS_A1 = 1e-6
NEGBIG = -1e30

_PHASES = ("att0", "ffn0", "att1", "ffn1")
_FUSED = True
_DEBUG = set()
_DBGHEAD = 0
_TRUNC = 99

_NMC = [0]

def _nm(base):
    _NMC[0] += 1
    return f"{base}_{_NMC[0]}"




class Prog:
    ENGS = ("pe", "dve", "act", "pool", "sp")
    OWN = {"pe": "pe", "dve": "dve", "act": "act", "pool": "pool", "sp": None}

    def __init__(self, nc, ctx):
        self.nc = nc
        self.q = {e: [] for e in self.ENGS}
        self.sem = {}
        self.cnt = {}
        self.lastw = {e: {} for e in self.ENGS}
        for s in ("lds", "ldp", "ldm", "ldb", "st", "stp", "pe", "dve",
                  "act", "pool"):
            self.sem[s] = ctx.enter_context(nc.semaphore(s))
            self.cnt[s] = 0

    def tok(self, s):
        return (s, self.cnt[s])

    def barrier(self):
        return (self.tok("pe"), self.tok("dve"), self.tok("act"),
                self.tok("pool"))

    @staticmethod
    def _flat(ws):
        for w in ws:
            if w is None:
                continue
            if isinstance(w, str):
                raise TypeError(f"malformed wait token: {ws!r}")
            if isinstance(w, tuple) and len(w) == 2 and isinstance(w[0], str):
                yield w
            else:
                yield from Prog._flat(w)

    def op(self, eng, meth, *args, waits=(), inc=None, incby=1, **kw):
        wl = []
        own = self.OWN[eng]
        for (s, v) in self._flat(waits):
            if v <= 0 or s == own:
                continue
            if self.lastw[eng].get(s, -1) >= v:
                continue
            self.lastw[eng][s] = v
            wl.append((self.sem[s], v))
        ih = None
        if inc is not None:
            self.cnt[inc] += incby
            ih = self.sem[inc]
        def play(e, wl=wl, meth=meth, args=args, kw=kw, ih=ih, incby=incby):
            for hh, v in wl:
                e.wait_ge(hh, v)
            r = getattr(e, meth)(*args, **kw)
            if ih is not None:
                r.then_inc(ih, incby)
        self.q[eng].append(play)

    def dbg(self, name, tiles, waits=()):
        for i, t in enumerate(tiles):
            shp = list(t.shape) if hasattr(t, "shape") else None
            ap = t if hasattr(t, "tensor") else t[:]
            d = self.nc.dram_tensor(_nm(f"dbg_{name}_{i}"), list(ap.shape),
                                    ap.dtype, kind="ExternalOutput")
            self.store("sp", d.ap(), ap, waits=waits)

    def load(self, eng, dst, src, waits=(), sem=None):
        if sem is None:
            sem = "lds" if eng == "sp" else "ldp"
        self.op(eng, "dma_start", waits=waits, inc=sem, incby=16,
                out=dst, in_=src)
        return self.tok(sem)

    def store(self, eng, dst, src, waits=()):
        sem = "st" if eng == "sp" else "stp"
        self.op(eng, "dma_start", waits=waits, inc=sem, incby=16,
                out=dst, in_=src)
        return self.tok(sem)

    def finalize(self, block):
        for name, attr in (("sp", "sync"), ("pe", "tensor"), ("dve", "vector"),
                           ("act", "scalar"), ("pool", "gpsimd")):
            ops = self.q[name]
            def run(e, ops=ops):
                for f in ops:
                    f(e)
            getattr(block, attr)(run)


def blocks512(q0, hi):
    out = []
    j0 = q0 // 512
    for j in range(j0, (hi + 511) // 512):
        lo = max(q0, j * 512)
        h2 = min(hi, (j + 1) * 512)
        if lo < h2:
            out.append((j, lo, h2))
    return out



def host_consts():
    c = {}
    d = np.arange(0, DK, 2, dtype=np.float64)
    inv = 10000.0 ** (-(d / DK))
    pos = np.arange(S, dtype=np.float64)
    ang = pos[:, None] * inv[None, :]
    cosr = np.repeat(np.cos(ang), 2, axis=1)
    sinr = np.repeat(np.sin(ang), 2, axis=1)
    cosf = cosr.T.astype(np.float32)
    sinf = sinr.T.astype(np.float32)
    c["c_cos2"] = np.ascontiguousarray(np.concatenate([cosf, cosf], axis=0))
    c["c_sin2"] = np.ascontiguousarray(np.concatenate([sinf, sinf], axis=0))
    m = np.zeros((DK, DK), np.float32)
    for i in range(0, DK, 2):
        m[i, i + 1] = -1.0
        m[i + 1, i] = 1.0
    m2 = np.zeros((128, 128), np.float32)
    m2[0:64, 0:64] = m.T
    m2[64:128, 64:128] = m.T
    c["c_mrot"] = m2
    tri = np.where(np.arange(128)[None, :] >= np.arange(128)[:, None],
                   0.0, NEGBIG).astype(np.float32)
    c["c_tri"] = np.ascontiguousarray(tri)
    c["c_idf"] = np.eye(128, dtype=np.float32)
    ones = np.zeros((128, 4), np.float32)
    ones[:, 0] = 1.0
    ones[0:64, 1] = 1.0
    ones[64:128, 2] = 1.0
    c["c_onesc"] = ones
    c["c_onesr"] = np.ones((1, 128), np.float32)
    return c


class CX:
    pass



def emit_ln(p, cx, src_chunks, a1, goff, sqt, rstg, src_ready):
    nc = p.nc
    ra, rb = cx.ra, cx.rb
    with ExitStack() as c2:
        psx = c2.enter_context(nc.psum_tensor(_nm("ln_psx"), [1, S], F32))
        psq = c2.enter_context(nc.psum_tensor(_nm("ln_psq"), [1, S], F32))
        bar = p.barrier()
        nst_ = len(rstg)
        stg_free = [bar] * nst_
        for ch in range(NCH):
            stg = rstg[ch % nst_]
            av = a1[ch][:, goff:goff + S]
            lt = p.load("sp", stg[:], src_chunks[ch],
                        waits=tuple(stg_free[ch % nst_]) + (src_ready,))
            p.op("dve", "tensor_copy", sqt[:], stg[:],
                 waits=(lt,) + tuple(bar), inc="dve")
            p.op("dve", "tensor_tensor", av, stg[:], sqt[:], ALU.subtract,
                 inc="dve")
            p.op("dve", "tensor_tensor", cx.sc1[:], stg[:], stg[:], ALU.mult,
                 waits=tuple(cx.free("sc1")), inc="dve")
            hl_tok = p.tok("dve")
            for nb4 in range(4):
                sl = slice(nb4 * 512, (nb4 + 1) * 512)
                p.op("pe", "matmul", psx.ap()[:, sl], cx.onesc[:, 0:1],
                     sqt[:, sl], waits=(hl_tok,), start=(ch == 0),
                     stop=False)
                p.op("pe", "matmul", psx.ap()[:, sl], cx.onesc[:, 0:1],
                     av[:, sl], start=False, stop=(ch == NCH - 1),
                     inc=("pe" if nb4 == 3 else None))
            hi_mm = p.tok("pe")
            p.op("dve", "tensor_copy", sqt[:], cx.sc1[:], waits=(hi_mm,),
                 inc="dve")
            p.op("dve", "tensor_tensor", av, cx.sc1[:], sqt[:], ALU.subtract,
                 inc="dve")
            sq_tok = p.tok("dve")
            cx.setfree("sc1", sq_tok)
            for nb4 in range(4):
                sl = slice(nb4 * 512, (nb4 + 1) * 512)
                p.op("pe", "matmul", psq.ap()[:, sl], cx.onesc[:, 0:1],
                     sqt[:, sl], waits=(sq_tok,), start=(ch == 0),
                     stop=False)
                p.op("pe", "matmul", psq.ap()[:, sl], cx.onesc[:, 0:1],
                     av[:, sl], start=False, stop=(ch == NCH - 1),
                     inc=("pe" if nb4 == 3 else None))
            stg_free[ch % nst_] = [p.tok("dve"), p.tok("pe")]
        stats_tok = p.tok("pe")
        p.op("dve", "tensor_scalar", ra[:], psx.ap()[:], 1.0 / DM, None,
             ALU.mult, waits=(stats_tok,) + tuple(cx.free("ra")), inc="dve")
        p.op("dve", "tensor_tensor", rb[:], ra[:], ra[:], ALU.mult,
             waits=tuple(cx.free("rb")), inc="dve")
        p.op("dve", "scalar_tensor_tensor", rb[:], psq.ap()[:], 1.0 / DM,
             rb[:], ALU.mult, ALU.subtract, inc="dve")
        p.op("dve", "tensor_scalar", rb[:], rb[:], EPS_LN, None, ALU.add,
             inc="dve")
        var_tok = p.tok("dve")
        p.op("pe", "nop", waits=(var_tok,))
    with ExitStack() as c2:
        mbp = c2.enter_context(nc.psum_tensor(_nm("ln_mb"), [128, S], F32))
        rbp = c2.enter_context(nc.psum_tensor(_nm("ln_rb"), [128, S], F32))
        for nb4 in range(4):
            sl = slice(nb4 * 512, (nb4 + 1) * 512)
            p.op("pe", "matmul", mbp.ap()[:, sl], cx.onesrf[:], ra[:, sl],
                 waits=(var_tok,), start=True, stop=True,
                 inc=("pe" if nb4 == 3 else None))
        mb_tok = p.tok("pe")
        p.op("dve", "reciprocal", ra[:], rb[:], waits=(mb_tok,), inc="dve")
        p.op("act", "activation", ra[:], ra[:], AF.Sqrt,
             waits=(p.tok("dve"),), inc="act")
        rstd_tok = p.tok("act")
        for nb4 in range(4):
            sl = slice(nb4 * 512, (nb4 + 1) * 512)
            p.op("pe", "matmul", rbp.ap()[:, sl], cx.onesrf[:], ra[:, sl],
                 waits=(rstd_tok,), start=True, stop=True,
                 inc=("pe" if nb4 == 3 else None))
        rb_tok = p.tok("pe")
        cx.setfree("ra", p.tok("pe"))
        cx.setfree("rb", p.tok("dve"))
        prev = [[p.tok("dve")]] * len(rstg)
        for ch in range(NCH):
            stg = rstg[ch % len(rstg)]
            lt = p.load("sp", stg[:], src_chunks[ch],
                        waits=tuple(prev[ch % len(rstg)]) + (src_ready,))
            p.op("dve", "tensor_tensor", cx.sc1[:], stg[:], mbp.ap()[:],
                 ALU.subtract, waits=(lt, mb_tok) + tuple(cx.free("sc1")),
                 inc="dve")
            p.op("dve", "tensor_tensor", a1[ch][:, goff:goff + S],
                 cx.sc1[:], rbp.ap()[:], ALU.mult, waits=(rb_tok,),
                 inc="dve")
            cx.setfree("sc1", p.tok("dve"))
            prev[ch % len(rstg)] = [p.tok("dve")]
        out_tok = p.tok("dve")
        p.op("pe", "nop", waits=(out_tok,))
    return out_tok


def emit_lnonly(p, cx, b, src, dst, src_ready):
    nc = p.nc
    with ExitStack() as ctx:
        sb = lambda n, shp, dt: ctx.enter_context(nc.sbuf_tensor(_nm(n), shp, dt))
        a1 = [sb(f"lo_a1_{c}", [128, S], BF16) for c in range(NCH)]
        sqt = sb("lo_sqt", [128, S], BF16)
        rstg = [sb("lo_rstg", [128, S], F32)]
        ln_tok = emit_ln(p, cx, src, a1, 0, sqt, rstg, src_ready)
        out_toks = []
        for c in range(NCH):
            p.op("dve", "tensor_copy", rstg[0][:], a1[c][:],
                 waits=(ln_tok,) + tuple(out_toks[-1:]), inc="dve")
            t = p.tok("dve")
            st = p.store("sp", dst[c], rstg[0][:], waits=(t,))
            out_toks.append(st)
    return p.tok("st")



def emit_att0(p, cx, b, src, dst, src_ready):
    nc = p.nc
    with ExitStack() as ctx:
        sb = lambda n, shp, dt: ctx.enter_context(nc.sbuf_tensor(_nm(n), shp, dt))
        a1 = [sb(f"a0_a1_{c}", [128, S], BF16) for c in range(NCH)]
        a2 = [sb(f"a0_a2_{c}", [128, 2080], BF16) for c in range(NCH)]
        ota = [sb(f"a0_ot_{c}", [128, S], BF16) for c in range(NCH)]
        sbt = sb("a0_sbt", [128, S], BF16)
        rstg = [sb("a0_rstg", [128, S], F32)]
        cos2 = sb("a0_cos", [128, S], F32)
        sin2 = sb("a0_sin", [128, S], F32)
        qksb = sb("a0_qksb", [128, S], BF16)
        qrp = sb("a0_qrp", [128, S], BF16)
        krp = sb("a0_krp", [128, S], BF16)
        rsbc = sb("a0_rsbc", [64, S], BF16)
        negc = sb("a0_negc", [128, 1], F32)
        segr = sb("a0_segr", [1, 16], F32)
        segr2 = sb("a0_segr2", [1, 16], F32)
        negcb = sb("a0_negcb", [1, 16], BF16)
        kmax = sb("a0_kmax", [1, 1], F32)
        wqp = [sb(f"a0_wqp{i}", [128, 1024], BF16) for i in range(2)]
        wkp = [sb(f"a0_wkp{i}", [128, 1024], BF16) for i in range(2)]
        wvh = sb("a0_wvh", [128, 4096], BF16)
        woc = [sb(f"a0_woc{i}", [128, 1024], BF16) for i in range(2)]

        bar0 = p.barrier()
        p.load("sp", cos2[:], cx.cos2_h, waits=tuple(bar0))
        p.load("sp", sin2[:], cx.sin2_h)
        ct = p.tok("lds")

        ln_tok = emit_ln(p, cx, src, a1, 0, sbt, rstg, src_ready)
        if b == 0 and "ln" in _DEBUG:
            p.dbg("ln", [t[:] for t in a1], waits=(ln_tok,))
        if b == 0 and "lnmid" in _DEBUG:
            p.dbg("lnmid", [t[:] for t in a1], waits=(ln_tok, v_tok_holder[0]))

        if _TRUNC < 2:
            return p.tok("st")
        ones_toks = []
        for c in range(NCH):
            p.op("dve", "memset", a2[c][:], 1.0, waits=tuple(bar0), inc="dve")
            ones_toks.append(p.tok("dve"))
        with ExitStack() as c2:
            pv = [c2.enter_context(nc.psum_tensor(_nm(f"a0_pv{i}"), [128, 512], F32))
                  for i in range(2)]
            wvh_free = tuple(bar0)
            for vh in range(2):
                wv_tok = p.load(
                    "pool",
                    wvh[:].rearrange("p (c n) -> p c n", c=8),
                    cx.wv_h[vh], waits=wvh_free)
                pv_free = [(), ()]
                for st in range(NST):
                    buf = st % 2
                    for c in range(NCH):
                        p.op("pe", "matmul", pv[buf].ap()[:],
                             a1[c][:, st * 128:(st + 1) * 128],
                             wvh[:, c * 512:(c + 1) * 512],
                             waits=(ln_tok, wv_tok) + tuple(pv_free[buf]),
                             start=(c == 0), stop=(c == NCH - 1),
                             inc=("pe" if c == NCH - 1 else None))
                    mm_tok = p.tok("pe")
                    dst65 = a2[st // 2][:, (st % 2) * 1040:
                                        (st % 2) * 1040 + 1040]
                    dv = dst65.rearrange("p (h k) -> p h k",
                                         k=65)[:, vh * 8:(vh + 1) * 8, 0:64]
                    p.op("dve", "tensor_copy", dv,
                         pv[buf].ap().rearrange("p (h k) -> p h k", k=64),
                         waits=(mm_tok, ones_toks[st // 2]), inc="dve")
                    pv_free[buf] = (p.tok("dve"),)
                wvh_free = (p.tok("pe"),)
            v_tok = p.tok("dve")
            p.op("pe", "nop", waits=(v_tok,))
        if b == 0 and "v" in _DEBUG:
            p.dbg("v", [t[:] for t in a2], waits=(v_tok,))

        if _TRUNC < 3:
            return p.tok("st")
        wfree = [tuple(bar0), tuple(bar0)]
        qrp_free = ()
        krp_free = ()
        sbt_free = (ln_tok,)
        prev_pair_done = ()
        for pr in range(H // 2):
            buf = pr % 2
            wq_tok = p.load("pool",
                            wqp[buf][:].rearrange("p (c o) -> p c o", c=8),
                            cx.wq_h[pr], waits=wfree[buf])
            wk_tok = p.load("pool",
                            wkp[buf][:].rearrange("p (c o) -> p c o", c=8),
                            cx.wk_h[pr], waits=wfree[buf])
            with ExitStack() as c2:
                pqk = c2.enter_context(nc.psum_tensor(_nm("a0_pqk"), [128, S], F32))
                prot = c2.enter_context(nc.psum_tensor(_nm("a0_prot"), [128, S],
                                                       F32))
                pqk_free = ()
                prot_free = ()
                for wt, wtok, dsttile, dfree in (
                        (wqp[buf], wq_tok, qrp, qrp_free),
                        (wkp[buf], wk_tok, krp, krp_free)):
                    for c in range(NCH):
                        for nb4 in range(4):
                            sl = slice(nb4 * 512, (nb4 + 1) * 512)
                            p.op("pe", "matmul", pqk.ap()[:, sl],
                                 wt[:, c * 128:(c + 1) * 128], a1[c][:, sl],
                                 waits=(ln_tok, wtok) + tuple(pqk_free),
                                 start=(c == 0), stop=(c == NCH - 1),
                                 inc=("pe" if (c == NCH - 1 and nb4 == 3)
                                      else None))
                    mm_tok = p.tok("pe")
                    p.op("dve", "tensor_copy", qksb[:], pqk.ap()[:],
                         waits=(mm_tok,), inc="dve")
                    qk_ev = p.tok("dve")
                    pqk_free = (qk_ev,)
                    for nb4 in range(4):
                        sl = slice(nb4 * 512, (nb4 + 1) * 512)
                        p.op("pe", "matmul", prot.ap()[:, sl], cx.mrot2[:],
                             qksb[:, sl],
                             waits=(qk_ev, ct) + tuple(prot_free),
                             start=True, stop=True,
                             inc=("pe" if nb4 == 3 else None))
                    rot_tok = p.tok("pe")
                    p.op("dve", "tensor_tensor", prot.ap()[:], prot.ap()[:],
                         sin2[:], ALU.mult, waits=(rot_tok, ct), inc="dve")
                    p.op("dve", "tensor_tensor", cx.sc1[:], qksb[:], cos2[:],
                         ALU.mult, waits=tuple(cx.free("sc1")), inc="dve")
                    p.op("dve", "tensor_tensor", dsttile[:], cx.sc1[:],
                         prot.ap()[:], ALU.add, waits=tuple(dfree), inc="dve")
                    prot_free = (p.tok("dve"),)
                    cx.setfree("sc1", p.tok("dve"))
                rope_tok = p.tok("dve")
                p.op("pe", "nop", waits=(rope_tok,))
            if b == 0 and pr == 0 and "qk" in _DEBUG:
                p.dbg("qk", [qrp[:], krp[:]], waits=(rope_tok,))
            head_toks = []
            for par in range(2):
                h = 2 * pr + par
                rows = slice(par * 64, (par + 1) * 64)
                with ExitStack() as c2:
                    pn = c2.enter_context(nc.psum_tensor(_nm("a0_pn"), [1, S], F32))
                    ec = cx.onesc[:, 1 + par:2 + par]
                    p.op("dve", "tensor_tensor", sbt[:], krp[:], krp[:],
                         ALU.mult, waits=(rope_tok,) + tuple(sbt_free),
                         inc="dve")
                    sq_tok = p.tok("dve")
                    for nb4 in range(4):
                        sl = slice(nb4 * 512, (nb4 + 1) * 512)
                        p.op("pe", "matmul", pn.ap()[:, sl], ec, sbt[:, sl],
                             waits=(sq_tok,), start=True, stop=True,
                             inc=("pe" if nb4 == 3 else None))
                    kn_tok = p.tok("pe")
                    p.op("dve", "reduce_max", kmax[:], pn.ap()[:], axis=AX.X,
                         waits=(kn_tok,))
                    p.op("dve", "tensor_tensor", sbt[:], qrp[:], qrp[:],
                         ALU.mult, waits=(kn_tok,), inc="dve")
                    kmax_tok = p.tok("dve")
                    sq_tok = p.tok("dve")
                    for nb4 in range(4):
                        sl = slice(nb4 * 512, (nb4 + 1) * 512)
                        p.op("pe", "matmul", pn.ap()[:, sl], ec, sbt[:, sl],
                             waits=(sq_tok, kmax_tok), start=True, stop=True,
                             inc=("pe" if nb4 == 3 else None))
                    qn_tok = p.tok("pe")
                    p.op("act", "mul", cx.ra[:], pn.ap()[:], kmax[:],
                         waits=(qn_tok, kmax_tok) + tuple(cx.free("ra")),
                         inc="act")
                    p.op("act", "activation", cx.ra[:], cx.ra[:], AF.Sqrt,
                         scale=float(1.0 / 64.0), inc="act")
                    crow_tok = p.tok("act")
                    p.op("dve", "reduce_max", negcb[:, 0:1], cx.ra[:],
                         axis=AX.X, negate=True, waits=(crow_tok,))
                    p.op("dve", "reduce_max", negcb[:, 1:2], cx.ra[:],
                         axis=AX.X, negate=True, inc="dve")
                    cx.setfree("ra", p.tok("dve"))
                    nb_tok = p.tok("dve")
                    pnc = c2.enter_context(nc.psum_tensor(_nm("a0_pnc"),
                                                          [128, 1], F32))
                    p.op("pe", "matmul", pnc.ap()[:], cx.onesrb[:],
                         negcb[:, 0:1], waits=(nb_tok,), start=True,
                         stop=True, inc="pe")
                    p.op("act", "activation", negc[:], pnc.ap()[:], AF.Copy,
                         waits=(p.tok("pe"),), inc="act")
                    negc_tok = p.tok("act")
                    p.op("pe", "nop", waits=(negc_tok,))
                if b == 0 and 2 * pr + par == _DBGHEAD and "negc" in _DEBUG:
                    p.dbg("negc", [negc[:]], waits=(negc_tok,))
                with ExitStack() as c2:
                    sct = c2.enter_context(nc.psum_tensor(_nm("a0_sct"), [128, S],
                                                          F32))
                    pot = c2.enter_context(nc.psum_tensor(_nm("a0_pot"), [65, S],
                                                          F32))
                    sct_free = ()
                    sbt_rd = ()
                    for si in range(NST):
                        q0 = si * 128
                        blks = blocks512(q0, S)
                        for (j, lo, hi2) in blks:
                            p.op("pe", "matmul", sct.ap()[:, lo:hi2],
                                 krp[rows, q0:q0 + 128], qrp[rows, lo:hi2],
                                 waits=(rope_tok,) + tuple(sct_free),
                                 start=True, stop=True,
                                 inc=("pe" if hi2 == S else None))
                        sc_tok = p.tok("pe")
                        p.op("dve", "tensor_tensor",
                             sct.ap()[:, q0:q0 + 128],
                             sct.ap()[:, q0:q0 + 128], cx.tri[:], ALU.add,
                             waits=(sc_tok,), inc="dve")
                        mask_tok = p.tok("dve")
                        p.op("act", "activation", sbt[:, q0:S],
                             sct.ap()[:, q0:S], AF.Exp,
                             bias=negc[:, 0:1], scale=0.125,
                             waits=(mask_tok, negc_tok) + tuple(sbt_rd),
                             inc="act")
                        exp_tok = p.tok("act")
                        sct_free = (exp_tok,)
                        if b == 0 and 2 * pr + par == _DBGHEAD and si == 0 \
                                and "probs0" in _DEBUG:
                            p.dbg("probs0", [sbt[:]], waits=(exp_tok,))
                        vt = a2[si // 2][:, (si % 2) * 1040:
                                         (si % 2) * 1040 + 1040]
                        lh = vt[:, h * 65:h * 65 + 65]
                        for (j, lo, hi2) in blks:
                            last_si = min(NST - 1, 4 * j + 3)
                            p.op("pe", "matmul", pot.ap()[:, lo:hi2], lh,
                                 sbt[:, lo:hi2], waits=(exp_tok, v_tok),
                                 start=(si == 0), stop=(si == last_si),
                                 inc=("pe" if (si == last_si and hi2 == S)
                                      else None))
                        sbt_rd = (p.tok("pe"),)
                    ot_tok = p.tok("pe")
                    sbt_free = sbt_rd
                    if b == 0 and 2 * pr + par == _DBGHEAD and "pot" in _DEBUG:
                        p.op("act", "activation", rstg[0][0:65, :],
                             pot.ap()[0:65, :], AF.Copy, waits=(ot_tok,),
                             inc="act")
                        p.dbg("pot", [rstg[0][0:65, :]],
                              waits=(p.tok("act"),))
                    p.op("act", "activation", cx.ra[:], pot.ap()[64:65, :],
                         AF.Copy, waits=(ot_tok,) + tuple(cx.free("ra")),
                         inc="act")
                    p.op("dve", "reciprocal", cx.rb[:], cx.ra[:],
                         waits=(p.tok("act"),) + tuple(cx.free("rb")),
                         inc="dve")
                    rs_tok = p.tok("dve")
                    cx.setfree("ra", rs_tok)
                    src_b = cx.rb[:].unsqueeze(1).broadcast_to([1, 64, S])
                    bt = p.load("pool", rsbc[:], src_b, waits=(rs_tok,),
                                sem="ldb")
                    cx.setfree("rb", bt)
                    if b == 0 and 2 * pr + par == _DBGHEAD \
                            and "rsdump" in _DEBUG:
                        p.dbg("rsra", [cx.ra[:]], waits=(p.tok("act"),))
                        p.dbg("rsrb", [cx.rb[:]], waits=(rs_tok,))
                        p.dbg("rsbc", [rsbc[:]], waits=(bt,))
                    p.op("dve", "tensor_tensor", ota[pr][rows, :],
                         pot.ap()[0:64, :], rsbc[:], ALU.mult,
                         waits=(bt, ot_tok), inc="dve")
                    head_toks.append(p.tok("dve"))
                    p.op("pe", "nop", waits=(head_toks[-1],))
                    if b == 0 and 2 * pr + par == _DBGHEAD \
                            and "otaev" in _DEBUG:
                        p.dbg("otaev", [ota[pr][:]], waits=(head_toks[-1],))
            qrp_free = tuple(head_toks)
            krp_free = tuple(head_toks)
            wfree[buf] = tuple(head_toks)
        ota_tok = p.tok("dve")
        if b == 0 and "ln2" in _DEBUG:
            p.dbg("ln2", [t[:] for t in a1], waits=(ota_tok,))
        if b == 0 and "ota" in _DEBUG:
            p.dbg("ota", [t[:] for t in ota], waits=(ota_tok,))

        if _TRUNC < 4:
            return p.tok("st")
        with ExitStack() as c2:
            pw = [c2.enter_context(nc.psum_tensor(_nm(f"a0_pw{i}"), [128, S], F32))
                  for i in range(2)]
            wo_free = [tuple(bar0), tuple(bar0)]
            pw_free = [(), ()]
            rs_free = [(ln_tok,)]
            for oc in range(NCH):
                buf = oc % 2
                wo_tok = p.load("pool",
                                woc[buf][:].rearrange("p (c o) -> p c o", c=8),
                                cx.wo_h[oc], waits=wo_free[buf])
                for c in range(NCH):
                    for nb4 in range(4):
                        sl = slice(nb4 * 512, (nb4 + 1) * 512)
                        p.op("pe", "matmul", pw[buf].ap()[:, sl],
                             woc[buf][:, c * 128:(c + 1) * 128],
                             ota[c][:, sl],
                             waits=(ota_tok, wo_tok) + tuple(pw_free[buf]),
                             start=(c == 0), stop=(c == NCH - 1),
                             inc=("pe" if (c == NCH - 1 and nb4 == 3)
                                  else None))
                mm_tok = p.tok("pe")
                wo_free[buf] = (mm_tok,)
                stg = rstg[0]
                lt = p.load("sp", stg[:], src[oc],
                            waits=tuple(rs_free[0]) + (src_ready,))
                p.op("dve", "tensor_tensor", stg[:], stg[:],
                     pw[buf].ap()[:], ALU.add, waits=(lt, mm_tok), inc="dve")
                pw_free[buf] = (p.tok("dve"),)
                st_tok = p.store("sp", dst[oc], stg[:], waits=(p.tok("dve"),))
                rs_free[0] = (p.tok("dve"), st_tok)
            p.op("pe", "nop", waits=(p.tok("dve"),))
    return p.tok("st")





def emit_ffn0(p, cx, b, src, dst, src_ready):
    nc = p.nc
    with ExitStack() as ctx:
        sb = lambda n, shp, dt: ctx.enter_context(nc.sbuf_tensor(_nm(n), shp, dt))
        a1 = [sb(f"f0_a1_{c}", [128, 2056], BF16) for c in range(NCH)]
        a2 = [sb(f"f0_a2_{c}", [128, S], BF16) for c in range(NCH)]
        a3 = [sb(f"f0_a3_{c}", [128, S], BF16) for c in range(NCH)]
        sbt = sb("f0_sbt", [128, S], BF16)
        rstg = [sb("f0_rstg", [128, S], F32)]
        mur = sb("f0_mur", [128, 8], F32)
        muk = sb("f0_muk", [128, 8], F32)
        wcol = [sb(f"f0_w{i}", [128, 1024], BF16) for i in range(2)]

        bar0 = p.barrier()
        for c in range(NCH):
            p.op("dve", "memset", a1[c][:, 0:1], 0.0, waits=tuple(bar0),
                 inc=("dve" if c == NCH - 1 else None))
        guard_tok = p.tok("dve")
        mu_toks = []
        for c in range(NCH):
            mu_toks.append(p.load("sp", mur[:, c:c + 1],
                                  cx.mur_h[c * 128:(c + 1) * 128].unsqueeze(1),
                                  waits=tuple(bar0)))
            mu_toks.append(p.load("sp", muk[:, c:c + 1],
                                  cx.muk_h[c * 128:(c + 1) * 128].unsqueeze(1)))
        mu_tok = mu_toks[-1]

        ln_tok = emit_ln(p, cx, src, a1, 1, sbt, rstg, src_ready)

        def emit_mix(mu, dep):
            toks = []
            for c in range(NCH):
                xc = a1[c][:, 1:1 + S]
                xp = a1[c][:, 0:S]
                p.op("dve", "tensor_tensor", cx.sc1[:], xc, xp, ALU.subtract,
                     waits=(ln_tok, guard_tok, mu_tok) + tuple(cx.free("sc1"))
                     + tuple(dep), inc="dve")
                p.op("dve", "tensor_scalar", cx.sc1[:], cx.sc1[:],
                     mu[:, c:c + 1], None, ALU.mult, inc="dve")
                p.op("dve", "tensor_tensor", a2[c][:], cx.sc1[:], xp, ALU.add,
                     inc="dve")
                cx.setfree("sc1", p.tok("dve"))
                toks.append(p.tok("dve"))
            return toks

        xr_toks = emit_mix(mur, ())

        def emit_proj(w_h, rhs_tiles, rhs_dep, evac, wdep0):
            with ExitStack() as c2:
                ps = [c2.enter_context(nc.psum_tensor(_nm("f0_ps"), [128, S],
                                                      F32)) for i in range(2)]
                wfree = [wdep0, wdep0]
                ps_free = [(), ()]
                for oc in range(NCH):
                    buf = oc % 2
                    wt = p.load("pool",
                                wcol[buf][:].rearrange("p (c o) -> p c o", c=8),
                                w_h[oc], waits=wfree[buf])
                    for c in range(NCH):
                        for nb4 in range(4):
                            sl = slice(nb4 * 512, (nb4 + 1) * 512)
                            p.op("pe", "matmul", ps[buf].ap()[:, sl],
                                 wcol[buf][:, c * 128:(c + 1) * 128],
                                 rhs_tiles[c], waits=(wt,) + tuple(rhs_dep)
                                 + tuple(ps_free[buf]),
                                 start=(c == 0), stop=(c == NCH - 1),
                                 inc=("pe" if (c == NCH - 1 and nb4 == 3)
                                      else None))
                    mm_tok = p.tok("pe")
                    wfree[buf] = (mm_tok,)
                    ev_tok = evac(oc, ps[buf], mm_tok)
                    ps_free[buf] = (ev_tok,)
                p.op("pe", "nop", waits=(p.tok("dve"), p.tok("act")))

        rhs_xrk = [a2[c][:].rearrange("p (nb s) -> p nb s", nb=4)
                   for c in range(NCH)]
        rhs_xrk = [a2[c][:] for c in range(NCH)]

        def evac_r(oc, ps, mm_tok):
            p.op("act", "activation", a3[oc][:], ps.ap()[:], AF.Sigmoid,
                 waits=(mm_tok,), inc="act")
            return p.tok("act")

        def rhs_sl(tiles):
            return tiles

        def emit_proj2(w_h, rhs_tiles, rhs_dep, evac, wdep0):
            with ExitStack() as c2:
                ps = [c2.enter_context(nc.psum_tensor(_nm("f0_ps"), [128, S],
                                                      F32)) for i in range(2)]
                wfree = [wdep0, wdep0]
                ps_free = [(), ()]
                for oc in range(NCH):
                    buf = oc % 2
                    wt = p.load("pool",
                                wcol[buf][:].rearrange("p (c o) -> p c o", c=8),
                                w_h[oc], waits=wfree[buf])
                    for c in range(NCH):
                        for nb4 in range(4):
                            sl = slice(nb4 * 512, (nb4 + 1) * 512)
                            p.op("pe", "matmul", ps[buf].ap()[:, sl],
                                 wcol[buf][:, c * 128:(c + 1) * 128],
                                 rhs_tiles[c][:, sl],
                                 waits=(wt,) + tuple(rhs_dep)
                                 + tuple(ps_free[buf]),
                                 start=(c == 0), stop=(c == NCH - 1),
                                 inc=("pe" if (c == NCH - 1 and nb4 == 3)
                                      else None))
                    mm_tok = p.tok("pe")
                    wfree[buf] = (mm_tok,)
                    ev_tok = evac(oc, ps[buf], mm_tok)
                    ps_free[buf] = (ev_tok,)
                p.op("pe", "nop", waits=(p.tok("dve"), p.tok("act")))

        emit_proj2(cx.wrt_h, [a2[c][:] for c in range(NCH)], xr_toks, evac_r,
                   tuple(bar0))
        r_tok = p.tok("act")

        xk_toks = emit_mix(muk, (p.tok("pe"),))

        def evac_k(oc, ps, mm_tok):
            p.op("dve", "tensor_scalar", cx.sc1[:], ps.ap()[:], 0.0, None,
                 ALU.max, waits=(mm_tok,) + tuple(cx.free("sc1")), inc="dve")
            p.op("dve", "tensor_tensor", a1[oc][:, 0:S], cx.sc1[:], cx.sc1[:],
                 ALU.mult, inc="dve")
            cx.setfree("sc1", p.tok("dve"))
            return p.tok("dve")

        emit_proj2(cx.wkt_h, [a2[c][:] for c in range(NCH)], xk_toks, evac_k,
                   (p.tok("pe"),))
        kact_tok = p.tok("dve")

        st_toks = {}
        rs_free = [(ln_tok,)]

        def evac_v(oc, ps, mm_tok):
            lt = p.load("sp", rstg[0][:], src[oc],
                        waits=tuple(rs_free[0]) + (src_ready,))
            p.op("dve", "tensor_tensor", cx.sc1[:], ps.ap()[:], a3[oc][:],
                 ALU.mult, waits=(mm_tok, r_tok) + tuple(cx.free("sc1")),
                 inc="dve")
            p.op("dve", "tensor_tensor", rstg[0][:], rstg[0][:], cx.sc1[:],
                 ALU.add, waits=(lt,), inc="dve")
            cx.setfree("sc1", p.tok("dve"))
            st = p.store("sp", dst[oc], rstg[0][:], waits=(p.tok("dve"),))
            rs_free[0] = (p.tok("dve"), st)
            return p.tok("dve")

        emit_proj2(cx.wvt_h, [a1[c][:, 0:S] for c in range(NCH)],
                   (kact_tok,), evac_v, (p.tok("pe"),))
    return p.tok("st")



def emit_att1(p, cx, b, src, dst, src_ready):
    nc = p.nc
    with ExitStack() as ctx:
        sb = lambda n, shp, dt: ctx.enter_context(nc.sbuf_tensor(_nm(n), shp, dt))
        a1 = [sb(f"t1_a1_{c}", [128, S], BF16) for c in range(NCH)]
        a2 = [sb(f"t1_a2_{c}", [128, 2080], BF16) for c in range(NCH)]
        a3 = [sb(f"t1_a3_{c}", [128, S], BF16) for c in range(NCH)]
        a4 = [sb(f"t1_a4_{c}", [128, S], BF16) for c in range(NCH)]
        sbt = sb("t1_sbt", [128, S], BF16)
        sc2 = sb("t1_sc2", [128, S], F32)
        rstg = [sb("t1_rstg", [128, S], F32)]
        bqc = sb("t1_bqc", [128, 8], F32)
        boc = sb("t1_boc", [128, 8], F32)
        bkb = sb("t1_bkb", [128, 1024], F32)
        bvb = sb("t1_bvb", [128, 1024], F32)
        wcol = [sb(f"t1_w{i}", [128, 1024], BF16) for i in range(2)]
        whalf = sb("t1_wh", [128, 4096], BF16)
        kvs = sb("t1_kvs", [128, 65], BF16)
        rsbc = sb("t1_rsbc", [64, S], BF16)

        bar0 = p.barrier()
        bias_toks = []
        for c in range(NCH):
            bias_toks.append(p.load("sp", bqc[:, c:c + 1],
                                    cx.bq_h[c * 128:(c + 1) * 128].unsqueeze(1),
                                    waits=tuple(bar0)))
            bias_toks.append(p.load("sp", boc[:, c:c + 1],
                                    cx.bo_h[c * 128:(c + 1) * 128].unsqueeze(1)))
        bk_tok = p.load("pool", bkb[:],
                        cx.bk_h.unsqueeze(0).unsqueeze(1)
                        .broadcast_to([1, 128, DM]), waits=tuple(bar0))
        bv_tok = p.load("pool", bvb[:],
                        cx.bv_h.unsqueeze(0).unsqueeze(1)
                        .broadcast_to([1, 128, DM]))
        bias_tok = bias_toks[-1]

        ln_tok = emit_ln(p, cx, src, a1, 0, sbt, rstg, src_ready)

        with ExitStack() as c2:
            pq = c2.enter_context(nc.psum_tensor(_nm("t1_pq"), [128, S], F32))
            wfree = [tuple(bar0), tuple(bar0)]
            ps_free = ()
            for pr in range(NCH):
                buf = pr % 2
                wt = p.load("pool",
                            wcol[buf][:].rearrange("p (c o) -> p c o", c=8),
                            cx.wq1_h[pr], waits=wfree[buf])
                for c in range(NCH):
                    for nb4 in range(4):
                        sl = slice(nb4 * 512, (nb4 + 1) * 512)
                        p.op("pe", "matmul", pq.ap()[:, sl],
                             wcol[buf][:, c * 128:(c + 1) * 128], a1[c][:, sl],
                             waits=(ln_tok, wt) + tuple(ps_free),
                             start=(c == 0), stop=(c == NCH - 1),
                             inc=("pe" if (c == NCH - 1 and nb4 == 3)
                                  else None))
                mm_tok = p.tok("pe")
                wfree[buf] = (mm_tok,)
                p.op("dve", "tensor_scalar", cx.sc1[:], pq.ap()[:],
                     bqc[:, pr:pr + 1], 0.0, ALU.add, ALU.min,
                     waits=(mm_tok, bias_tok) + tuple(cx.free("sc1")),
                     inc="dve")
                p.op("dve", "tensor_scalar", sc2[:], pq.ap()[:],
                     bqc[:, pr:pr + 1], 0.0, ALU.add, ALU.max,
                     waits=tuple(cx.free("sc2")), inc="dve")
                z_tok = p.tok("dve")
                ps_free = (z_tok,)
                p.op("act", "activation", cx.sc1[:], cx.sc1[:], AF.Exp,
                     waits=(z_tok,), inc="act")
                e_tok = p.tok("act")
                p.op("dve", "tensor_tensor", a3[pr][:], cx.sc1[:], sc2[:],
                     ALU.add, waits=(e_tok,), inc="dve")
                cx.setfree("sc1", p.tok("dve"))
                cx.setfree("sc2", p.tok("dve"))
            qt_tok = p.tok("dve")
            p.op("pe", "nop", waits=(qt_tok,))

        ones_toks = []
        for c in range(NCH):
            p.op("dve", "memset", a2[c][:], 1.0, waits=tuple(bar0), inc="dve")
            ones_toks.append(p.tok("dve"))

        def tok_proj(w_h2, out_cb, brow, brow_tok, wfree0):
            with ExitStack() as c2:
                pv = [c2.enter_context(nc.psum_tensor(_nm("t1_pv"),
                                                      [128, 512], F32))
                      for i in range(2)]
                wh_free = wfree0
                for vh in range(2):
                    wt = p.load("pool",
                                whalf[:].rearrange("p (c n) -> p c n", c=8),
                                w_h2[vh], waits=wh_free)
                    pv_free = [(), ()]
                    for st in range(NST):
                        buf = st % 2
                        for c in range(NCH):
                            p.op("pe", "matmul", pv[buf].ap()[:],
                                 a1[c][:, st * 128:(st + 1) * 128],
                                 whalf[:, c * 512:(c + 1) * 512],
                                 waits=(ln_tok, wt) + tuple(pv_free[buf]),
                                 start=(c == 0), stop=(c == NCH - 1),
                                 inc=("pe" if c == NCH - 1 else None))
                        mm_tok = p.tok("pe")
                        ev = out_cb(vh, st, pv[buf], mm_tok, brow, brow_tok)
                        pv_free[buf] = (ev,)
                    wh_free = (p.tok("pe"),)
                p.op("pe", "nop", waits=(p.tok("dve"), p.tok("act")))

        def k_out(vh, st, ps, mm_tok, brow, brow_tok):
            cols = slice(vh * 512, (vh + 1) * 512)
            dstt = a4[st // 2][:, (st % 2) * 1024 + vh * 512:
                               (st % 2) * 1024 + (vh + 1) * 512]
            p.op("dve", "tensor_tensor", cx.sc1[:, 0:512], ps.ap()[:],
                 brow[:, cols], ALU.add,
                 waits=(mm_tok, brow_tok) + tuple(cx.free("sc1")), inc="dve")
            p.op("dve", "tensor_scalar", sc2[:, 0:512], cx.sc1[:, 0:512],
                 0.0, None, ALU.min, waits=tuple(cx.free("sc2")), inc="dve")
            z_tok = p.tok("dve")
            p.op("act", "activation", sc2[:, 0:512], sc2[:, 0:512], AF.Exp,
                 waits=(z_tok,), inc="act")
            e_tok = p.tok("act")
            p.op("dve", "tensor_scalar", cx.sc1[:, 0:512], cx.sc1[:, 0:512],
                 0.0, None, ALU.max, inc="dve")
            p.op("dve", "tensor_tensor", dstt, sc2[:, 0:512],
                 cx.sc1[:, 0:512], ALU.add, waits=(e_tok,), inc="dve")
            cx.setfree("sc1", p.tok("dve"))
            cx.setfree("sc2", p.tok("dve"))
            return p.tok("dve")

        def v_out(vh, st, ps, mm_tok, brow, brow_tok):
            cols = slice(vh * 512, (vh + 1) * 512)
            dst65 = a2[st // 2][:, (st % 2) * 1040:(st % 2) * 1040 + 1040]
            dv = dst65.rearrange("p (h k) -> p h k",
                                 k=65)[:, vh * 8:(vh + 1) * 8, 0:64]
            p.op("dve", "tensor_tensor", dv,
                 ps.ap()[:].rearrange("p (h k) -> p h k", k=64),
                 brow[:, cols].rearrange("p (h k) -> p h k", k=64), ALU.add,
                 waits=(mm_tok, brow_tok, ones_toks[st // 2]), inc="dve")
            return p.tok("dve")

        tok_proj(cx.wk1_h, k_out, bkb, bk_tok, tuple(bar0))
        k_tok = p.tok("dve")
        tok_proj(cx.wv1_h, v_out, bvb, bv_tok, (p.tok("pe"),))
        v_tok = p.tok("dve")

        for h in range(H):
            pr, par = h // 2, h % 2
            rows = slice(par * 64, (par + 1) * 64)
            with ExitStack() as c2:
                pkv = c2.enter_context(nc.psum_tensor(_nm("t1_pkv"),
                                                      [64, 65], F32))
                for si in range(NST):
                    lh = a4[si // 2][:, (si % 2) * 1024 + h * 64:
                                     (si % 2) * 1024 + h * 64 + 64]
                    vt = a2[si // 2][:, (si % 2) * 1040:(si % 2) * 1040 + 1040]
                    rh = vt[:, h * 65:h * 65 + 65]
                    p.op("pe", "matmul", pkv.ap()[:], lh, rh,
                         waits=(k_tok, v_tok), start=(si == 0),
                         stop=(si == NST - 1),
                         inc=("pe" if si == NST - 1 else None))
                kv_tok = p.tok("pe")
                kvdst = kvs[rows.start:rows.start + 64, :] if False else                     kvs[par * 64:(par + 1) * 64, :]
                p.op("dve", "tensor_copy", kvdst, pkv.ap()[:],
                     waits=(kv_tok,), inc="dve")
                kvs_tok = p.tok("dve")
                p.op("pe", "nop", waits=(kvs_tok,))
            with ExitStack() as c2:
                pot = c2.enter_context(nc.psum_tensor(_nm("t1_pot"),
                                                      [65, S], F32))
                for nb4 in range(4):
                    sl = slice(nb4 * 512, (nb4 + 1) * 512)
                    p.op("pe", "matmul", pot.ap()[:, sl],
                         kvs[par * 64:(par + 1) * 64, :],
                         a3[pr][rows, sl], waits=(kvs_tok, qt_tok),
                         start=True, stop=True,
                         inc=("pe" if nb4 == 3 else None))
                ot_tok = p.tok("pe")
                p.op("act", "activation", cx.ra[:], pot.ap()[64:65, :],
                     AF.Copy, waits=(ot_tok,) + tuple(cx.free("ra")),
                     inc="act")
                p.op("dve", "tensor_scalar", cx.rb[:], cx.ra[:], EPS_A1,
                     None, ALU.max, waits=(p.tok("act"),)
                     + tuple(cx.free("rb")), inc="dve")
                p.op("dve", "reciprocal", cx.rb[:], cx.rb[:], inc="dve")
                rs_tok = p.tok("dve")
                cx.setfree("ra", rs_tok)
                src_b = cx.rb[:].unsqueeze(1).broadcast_to([1, 64, S])
                bt = p.load("pool", rsbc[:], src_b, waits=(rs_tok,),
                            sem="ldb")
                cx.setfree("rb", bt)
                p.op("dve", "tensor_tensor", a1[pr][rows, :],
                     pot.ap()[0:64, :], rsbc[:], ALU.mult,
                     waits=(bt, ot_tok), inc="dve")
                p.op("pe", "nop", waits=(p.tok("dve"),))
        ota_tok = p.tok("dve")

        with ExitStack() as c2:
            pw = [c2.enter_context(nc.psum_tensor(_nm("t1_pw"), [128, S],
                                                  F32)) for i in range(2)]
            wo_free = [tuple(bar0), tuple(bar0)]
            pw_free = [(), ()]
            rs_free = [(ln_tok,)]
            for oc in range(NCH):
                buf = oc % 2
                wo_tok = p.load("pool",
                                wcol[buf][:].rearrange("p (c o) -> p c o",
                                                       c=8),
                                cx.wo1_h[oc], waits=wo_free[buf])
                for c in range(NCH):
                    for nb4 in range(4):
                        sl = slice(nb4 * 512, (nb4 + 1) * 512)
                        p.op("pe", "matmul", pw[buf].ap()[:, sl],
                             wcol[buf][:, c * 128:(c + 1) * 128], a1[c][:, sl],
                             waits=(ota_tok, wo_tok) + tuple(pw_free[buf]),
                             start=(c == 0), stop=(c == NCH - 1),
                             inc=("pe" if (c == NCH - 1 and nb4 == 3)
                                  else None))
                mm_tok = p.tok("pe")
                wo_free[buf] = (mm_tok,)
                p.op("dve", "tensor_scalar", cx.sc1[:], pw[buf].ap()[:],
                     boc[:, oc:oc + 1], None, ALU.add,
                     waits=(mm_tok, bias_tok) + tuple(cx.free("sc1")),
                     inc="dve")
                pw_free[buf] = (p.tok("dve"),)
                lt = p.load("sp", rstg[0][:], src[oc],
                            waits=tuple(rs_free[0]) + (src_ready,))
                p.op("dve", "tensor_tensor", rstg[0][:], rstg[0][:],
                     cx.sc1[:], ALU.add, waits=(lt,), inc="dve")
                cx.setfree("sc1", p.tok("dve"))
                st_tok = p.store("sp", dst[oc], rstg[0][:],
                                 waits=(p.tok("dve"),))
                rs_free[0] = (p.tok("dve"), st_tok)
            p.op("pe", "nop", waits=(p.tok("dve"),))
    return p.tok("st")





def emit_quant(p, cx, src_f32, dst_bf16, F, q, waits):
    nb = F // 16
    waits = tuple(waits)
    p.op("dve", "reduce_max", q["bm"][:, 0:nb],
         src_f32.rearrange("p (b i) -> p b i", i=16), axis=AX.X,
         apply_absolute_value=True, waits=waits)
    p.op("dve", "scalar_tensor_tensor", q["ab"][:, 0:F], src_f32, -1.0,
         src_f32, ALU.mult, ALU.max, waits=waits, inc="dve")
    bm_tok = p.tok("dve")
    p.op("act", "mul", q["m2"][:, 0:nb], q["bm"][:, 0:nb], 1.0 / 6.0,
         waits=(bm_tok,))
    p.op("act", "activation", q["aa"][:, 0:F], src_f32, AF.Abs,
         inc="act")
    t2_tok = p.tok("act")
    p.op("dve", "tensor_scalar", q["m3"][:, 0:nb], q["m2"][:, 0:nb], 1e-12,
         None, ALU.max, waits=(t2_tok,))
    p.op("dve", "tensor_scalar", q["vv"][:, 0:F], src_f32, 0.0, 2.0,
         ALU.is_gt, ALU.mult, inc="dve")
    p.op("dve", "reciprocal", q["r6"][:, 0:nb], q["m3"][:, 0:nb])
    p.op("dve", "tensor_scalar", q["hs"][:, 0:nb], q["m3"][:, 0:nb], 0.5,
         None, ALU.mult)
    p.op("dve", "tensor_scalar", q["uu"][:, 0:F], q["ab"][:, 0:F], 1.0,
         None, ALU.mult, inc="dve")
    rsb = q["r6"][:, 0:nb].unsqueeze(2).broadcast_to([128, nb, 16])
    hsb = q["hs"][:, 0:nb].unsqueeze(2).broadcast_to([128, nb, 16])
    v3 = lambda t: t[:, 0:F].rearrange("p (b i) -> p b i", i=16)
    p.op("dve", "tensor_tensor", v3(q["aa"]), v3(q["ab"]), rsb, ALU.mult,
         inc="dve")
    aa = q["aa"][:, 0:F]
    p.op("dve", "tensor_scalar", q["uu"][:, 0:F], aa, 0.25, None, ALU.is_gt,
         inc="dve")
    for thr in (0.75, 1.25, 1.75):
        p.op("dve", "scalar_tensor_tensor", q["uu"][:, 0:F], aa, thr,
             q["uu"][:, 0:F], ALU.is_gt, ALU.add, inc="dve")
    p.op("dve", "tensor_scalar", q["ab"][:, 0:F], aa, 2.5, None, ALU.is_gt,
         inc="dve")
    p.op("dve", "scalar_tensor_tensor", q["ab"][:, 0:F], aa, 3.5,
         q["ab"][:, 0:F], ALU.is_gt, ALU.add, inc="dve")
    p.op("dve", "tensor_scalar", aa, aa, 5.0, 4.0, ALU.is_gt, ALU.mult,
         inc="dve")
    p.op("dve", "scalar_tensor_tensor", q["ab"][:, 0:F], q["ab"][:, 0:F],
         2.0, aa, ALU.mult, ALU.add, inc="dve")
    p.op("dve", "tensor_tensor", q["uu"][:, 0:F], q["uu"][:, 0:F],
         q["ab"][:, 0:F], ALU.add, inc="dve")
    p.op("dve", "scalar_tensor_tensor", q["uu"][:, 0:F], q["vv"][:, 0:F],
         1.0, q["uu"][:, 0:F], ALU.subtract, ALU.mult, inc="dve")
    p.op("dve", "tensor_tensor", v3x(dst_bf16, F), v3(q["uu"]), hsb,
         ALU.mult, inc="dve")
    return p.tok("dve")


def v3x(ap, F):
    return ap.rearrange("p (b i) -> p b i", i=16)


def emit_wqcache(p, cx):
    nc = p.nc
    with ExitStack() as ctx:
        sb = lambda n, shp, dt: ctx.enter_context(nc.sbuf_tensor(_nm(n), shp, dt))
        K = 2816
        WSTG = sb("wq_stg", [128, K], F32)
        WQ0 = sb("wq_q", [128, K], BF16)
        TSTG = sb("wq_t", [128, 128], BF16)
        q = {
            "ab": sb("wq_ab", [128, K], F32),
            "aa": sb("wq_aa", [128, K], F32),
            "uu": sb("wq_uu", [128, K], F32),
            "vv": sb("wq_vv", [128, K], F32),
            "m1": sb("wq_m1", [128, K // 2], F32),
            "m2": sb("wq_m2", [128, K // 4], F32),
            "m3": sb("wq_m3", [128, K // 8], F32),
            "bm": sb("wq_bm", [128, K // 16], F32),
            "r6": sb("wq_r6", [128, K // 16], F32),
            "rs": sb("wq_rs", [128, K // 16], F32),
            "hs": sb("wq_hs", [128, K // 16], F32),
        }
        with ExitStack() as c2:
            ptw = c2.enter_context(nc.psum_tensor(_nm("wq_pt"), [128, 128],
                                                  BF16))
            for (src_ap, dst_ap, N, Km) in (
                    (cx.wg_h, cx.wgt_h, IFF, DM),
                    (cx.wu_h, cx.wut_h, IFF, DM),
                    (cx.wd_h, cx.wdt_h, DM, IFF)):
                nr, nk = N // 128, Km // 128
                prev = p.barrier()
                for rc in range(nr):
                    lt = p.load("sp", WSTG[:, 0:Km],
                                src_ap[rc * 128:(rc + 1) * 128, :],
                                waits=tuple(prev))
                    qt = emit_quant(p, cx, WSTG[:, 0:Km], WQ0[:, 0:Km], Km,
                                    q, (lt,))
                    ev = ()
                    for kj in range(nk):
                        p.op("pe", "transpose", ptw.ap()[:],
                             WQ0[:, kj * 128:(kj + 1) * 128], cx.idb[:],
                             waits=(qt,) + tuple(ev), inc="pe")
                        p.op("dve", "tensor_copy", TSTG[:], ptw.ap()[:],
                             waits=(p.tok("pe"),), inc="dve")
                        st = p.store("sp",
                                     dst_ap[kj * 128:(kj + 1) * 128,
                                            rc * 128:(rc + 1) * 128],
                                     TSTG[:], waits=(p.tok("dve"),))
                        ev = (p.tok("dve"), st)
                    prev = (qt, p.tok("dve"), st)
            p.op("pe", "nop", waits=(p.tok("dve"),))
    cx.wqc_tok = p.tok("st")


def emit_ffn1(p, cx, b, src, dst, src_ready):
    nc = p.nc
    if b == 0 and cx.wqc_tok is None:
        emit_wqcache(p, cx)
    with ExitStack() as ctx:
        sb = lambda n, shp, dt: ctx.enter_context(nc.sbuf_tensor(_nm(n), shp, dt))
        a1 = [sb(f"f1_a1_{c}", [128, S], BF16) for c in range(NCH)]
        hqt = [sb(f"f1_hq_{c}", [128, S], BF16) for c in range(11)]
        RS8 = sb("f1_rs8", [128, 1024], F32)
        XT0 = sb("f1_xt0", [128, 1024], F32)
        XQ0 = sb("f1_xq0", [128, 1024], BF16)
        HQ0 = sb("f1_hq0", [128, 512], BF16)
        rstg = [sb("f1_rstg", [128, S], F32)]
        wg = sb("f1_wg", [128, 8 * 512], BF16)
        wu = sb("f1_wu", [128, 8 * 512], BF16)
        wd = [sb(f"f1_wd{i}", [128, 11 * 128], BF16) for i in range(2)]
        sumx = sb("f1_sumx", [128, 1], F32)
        sumq = sb("f1_sumq", [128, 1], F32)
        mcol = sb("f1_mcol", [128, 1], F32)
        vcol = sb("f1_vcol", [128, 1], F32)
        rcol = sb("f1_rcol", [128, 1], F32)
        FQ = 1024
        q = {
            "ab": sb("f1_ab", [128, FQ], F32),
            "aa": sb("f1_aa", [128, FQ], F32),
            "uu": sb("f1_uu", [128, FQ], F32),
            "vv": sb("f1_vv", [128, FQ], F32),
            "m1": sb("f1_m1", [128, FQ // 2], F32),
            "m2": sb("f1_m2", [128, FQ // 4], F32),
            "m3": sb("f1_m3", [128, FQ // 8], F32),
            "bm": sb("f1_bm", [128, FQ // 16], F32),
            "r6": sb("f1_r6", [128, FQ // 16], F32),
            "rs": sb("f1_rs", [128, FQ // 16], F32),
            "hs": sb("f1_hs", [128, FQ // 16], F32),
        }
        bar0 = p.barrier()

        with ExitStack() as c2:
            pxt = c2.enter_context(nc.psum_tensor(_nm("f1_pxt"), [128, 1024],
                                                  F32))
            pxq = c2.enter_context(nc.psum_tensor(_nm("f1_pxq"), [128, 1024],
                                                  BF16))
            prev = tuple(bar0) + (src_ready, cx.wqc_tok)
            for st in range(NST):
                lts = []
                for c in range(NCH):
                    lts.append(p.load("sp", RS8[:, c * 128:(c + 1) * 128],
                                      src[c][:, st * 128:(st + 1) * 128],
                                      waits=prev))
                for c in range(NCH):
                    p.op("pe", "transpose", pxt.ap()[:, c * 128:(c + 1) * 128],
                         RS8[:, c * 128:(c + 1) * 128], cx.idf[:],
                         waits=(lts[-1],) + prev,
                         inc=("pe" if c == NCH - 1 else None))
                tr_tok = p.tok("pe")
                p.op("act", "activation", XT0[:], pxt.ap()[:], AF.Copy,
                     waits=(tr_tok,), inc="act")
                x_tok = p.tok("act")
                p.op("dve", "reduce_sum", sumx[:], XT0[:], axis=AX.X,
                     waits=(x_tok,), inc="dve")
                p.op("dve", "tensor_tensor", q["ab"][:], XT0[:], XT0[:],
                     ALU.mult, inc="dve")
                p.op("dve", "reduce_sum", sumq[:], q["ab"][:], axis=AX.X)
                p.op("dve", "tensor_scalar", q["aa"][:], XT0[:], 1.0, None,
                     ALU.mult, inc="dve")
                s_tok = p.tok("dve")
                p.op("act", "mul", mcol[:], sumx[:], 1.0 / DM,
                     waits=(s_tok,), inc="act")
                p.op("act", "mul", vcol[:], sumq[:], 1.0 / DM, inc="act")
                p.op("act", "activation", rcol[:], mcol[:], AF.Square,
                     inc="act")
                m_tok = p.tok("act")
                p.op("dve", "tensor_tensor", vcol[:], vcol[:], rcol[:],
                     ALU.subtract, waits=(m_tok,), inc="dve")
                p.op("act", "activation", vcol[:], vcol[:], AF.Relu,
                     bias=cx.epsc[:, 0:1], waits=(p.tok("dve"),), inc="act")
                p.op("dve", "reciprocal", rcol[:], vcol[:],
                     waits=(p.tok("act"),), inc="dve")
                p.op("act", "activation", rcol[:], rcol[:], AF.Sqrt,
                     waits=(p.tok("dve"),), inc="act")
                r_tok = p.tok("act")
                p.op("dve", "tensor_scalar", XT0[:], XT0[:], mcol[:], rcol[:],
                     ALU.subtract, ALU.mult, waits=(r_tok,), inc="dve")
                y_tok = p.tok("dve")
                qt = emit_quant(p, cx, XT0[:], XQ0[:], 1024, q, (y_tok,))
                evs = ()
                for c in range(NCH):
                    p.op("pe", "transpose", pxq.ap()[:, c * 128:(c + 1) * 128],
                         XQ0[:, c * 128:(c + 1) * 128], cx.idb[:],
                         waits=(qt,) + tuple(evs),
                         inc=("pe" if c == NCH - 1 else None))
                tq_tok = p.tok("pe")
                for c in range(NCH):
                    p.op("dve", "tensor_copy",
                         a1[c][:, st * 128:(st + 1) * 128],
                         pxq.ap()[:, c * 128:(c + 1) * 128],
                         waits=(tq_tok,),
                         inc=("dve" if c == NCH - 1 else None))
                evs = (p.tok("dve"),)
                prev = (p.tok("dve"), p.tok("act"))
            xq_tok = p.tok("dve")
            p.op("pe", "nop", waits=(xq_tok,))
        if b == 0 and "xq" in _DEBUG:
            p.dbg("xq", [t[:] for t in a1], waits=(xq_tok,))

        oc_st = {}
        for nh in range(2):
            nbase = nh * 1408
            h_toks = []
            with ExitStack() as c2:
                pg = c2.enter_context(nc.psum_tensor(_nm("f1_pg"), [128, 512],
                                                     F32))
                pu = c2.enter_context(nc.psum_tensor(_nm("f1_pu"), [128, 512],
                                                     F32))
                pht = c2.enter_context(nc.psum_tensor(_nm("f1_pht"),
                                                      [128, 512], BF16))
                wfree = tuple(bar0) + (cx.wqc_tok,)
                g_free = ()
                for (nbo, nbw) in ((0, 512), (512, 512), (1024, 384)):
                    wgt = p.load("pool",
                                 wg[:, 0:8 * nbw].rearrange(
                                     "p (c n) -> p c n", c=8),
                                 cx.wgt_h[:, nbase + nbo:nbase + nbo + nbw]
                                 .rearrange("(c p1) n -> p1 c n", p1=128),
                                 waits=wfree)
                    wut = p.load("pool",
                                 wu[:, 0:8 * nbw].rearrange(
                                     "p (c n) -> p c n", c=8),
                                 cx.wut_h[:, nbase + nbo:nbase + nbo + nbw]
                                 .rearrange("(c p1) n -> p1 c n", p1=128),
                                 waits=wfree)
                    for st in range(NST):
                        for c in range(NCH):
                            p.op("pe", "matmul", pg.ap()[:, 0:nbw],
                                 a1[c][:, st * 128:(st + 1) * 128],
                                 wg[:, c * nbw:(c + 1) * nbw],
                                 waits=(xq_tok, wgt) + tuple(g_free),
                                 start=(c == 0), stop=(c == NCH - 1))
                            p.op("pe", "matmul", pu.ap()[:, 0:nbw],
                                 a1[c][:, st * 128:(st + 1) * 128],
                                 wu[:, c * nbw:(c + 1) * nbw],
                                 waits=(wut,), start=(c == 0),
                                 stop=(c == NCH - 1),
                                 inc=("pe" if c == NCH - 1 else None))
                        mm_tok = p.tok("pe")
                        p.op("act", "activation", cx.sc1[:, 0:nbw],
                             pg.ap()[:, 0:nbw], AF.Silu,
                             waits=(mm_tok,) + tuple(cx.free("sc1")),
                             inc="act")
                        p.op("dve", "tensor_tensor", cx.sc1[:, 0:nbw],
                             cx.sc1[:, 0:nbw], pu.ap()[:, 0:nbw], ALU.mult,
                             waits=(p.tok("act"),), inc="dve")
                        h_tok = p.tok("dve")
                        g_free = (h_tok,)
                        if b == 0 and nh == 0 and nbo == 0 and st == 0 \
                                and "hdump" in _DEBUG:
                            p.dbg("hd", [cx.sc1[:, 0:nbw]], waits=(h_tok,))
                        qt = emit_quant(p, cx, cx.sc1[:, 0:nbw], HQ0[:, 0:nbw],
                                        nbw, q, (h_tok,))
                        cx.setfree("sc1", qt)
                        if b == 0 and nh == 0 and nbo == 0 and st < 4 \
                                and "hq0" in _DEBUG:
                            p.dbg(f"hq0st{st}", [HQ0[:, 0:nbw]], waits=(qt,))
                            p.dbg(f"hsrc{st}", [cx.sc1[:, 0:nbw]],
                                  waits=(qt,))
                        evs = ()
                        for j in range(nbw // 128):
                            p.op("pe", "transpose",
                                 pht.ap()[:, j * 128:(j + 1) * 128],
                                 HQ0[:, j * 128:(j + 1) * 128], cx.idb[:],
                                 waits=(qt,) + tuple(evs),
                                 inc=("pe" if j == nbw // 128 - 1 else None))
                        th_tok = p.tok("pe")
                        for j in range(nbw // 128):
                            p.op("dve", "tensor_copy",
                                 hqt[(nbo + j * 128) // 128]
                                 [:, st * 128:(st + 1) * 128],
                                 pht.ap()[:, j * 128:(j + 1) * 128],
                                 waits=(th_tok,),
                                 inc=("dve" if j == nbw // 128 - 1 else None))
                        evs = (p.tok("dve"),)
                        h_toks.append(p.tok("dve"))
                    wfree = (p.tok("pe"),)
                p.op("pe", "nop", waits=(p.tok("dve"),))
            hq_tok = p.tok("dve")
            if b == 0 and "hqt" in _DEBUG:
                p.dbg(f"hqt{nh}", [t[:] for t in hqt], waits=(hq_tok,))
            with ExitStack() as c2:
                pd = [c2.enter_context(nc.psum_tensor(_nm("f1_pd"), [128, S],
                                                      F32)) for i in range(2)]
                wd_free = [(cx.wqc_tok,), (cx.wqc_tok,)]
                pd_free = [(), ()]
                rs_free = (p.tok("dve"),)
                for oc in range(NCH):
                    buf = oc % 2
                    wdt = p.load("pool",
                                 wd[buf][:].rearrange("p (c o) -> p c o",
                                                      c=11),
                                 cx.wdt_h[nh * 1408:(nh + 1) * 1408,
                                          oc * 128:(oc + 1) * 128]
                                 .rearrange("(c p1) o -> p1 c o", p1=128),
                                 waits=wd_free[buf])

                    for c in range(11):
                        for nb4 in range(4):
                            sl = slice(nb4 * 512, (nb4 + 1) * 512)
                            p.op("pe", "matmul", pd[buf].ap()[:, sl],
                                 wd[buf][:, c * 128:(c + 1) * 128],
                                 hqt[c][:, sl],
                                 waits=(hq_tok, wdt) + tuple(pd_free[buf]),
                                 start=(c == 0), stop=(c == 10),
                                 inc=("pe" if (c == 10 and nb4 == 3)
                                      else None))
                    mm_tok = p.tok("pe")
                    wd_free[buf] = (mm_tok,)
                    if b == 0 and nh == 0 and oc == 0 and "pd" in _DEBUG:
                        p.op("act", "activation", rstg[0][:],
                             pd[buf].ap()[:], AF.Copy, waits=(mm_tok,),
                             inc="act")
                        p.dbg("pd", [rstg[0][:]], waits=(p.tok("act"),))
                    rsrc = src[oc] if nh == 0 else dst[oc]
                    w8 = (src_ready,) if nh == 0 else (oc_st[oc],)
                    lt = p.load("sp", rstg[0][:], rsrc,
                                waits=tuple(rs_free) + w8)
                    p.op("dve", "tensor_tensor", rstg[0][:], rstg[0][:],
                         pd[buf].ap()[:], ALU.add, waits=(lt, mm_tok),
                         inc="dve")
                    pd_free[buf] = (p.tok("dve"),)
                    oc_st[oc] = p.store("sp", dst[oc], rstg[0][:],
                                        waits=(p.tok("dve"),))
                    rs_free = (p.tok("dve"), oc_st[oc])
                p.op("pe", "nop", waits=(p.tok("dve"),))
    return p.tok("st")



def build_program(phases):
    nc = bass.Bass("TRN2", target_bir_lowering=False,
                   detect_race_conditions=False)
    topctx = ExitStack()
    nc.ctx.enter_context(topctx)

    xin = nc.dram_tensor("xin", [NB, NCH, 128, S], F32, kind="ExternalInput")
    y = nc.dram_tensor("y", [NB, NCH, 128, S], F32, kind="ExternalOutput")
    cos2_h = nc.dram_tensor("c_cos2", [128, S], F32, kind="ExternalInput")
    sin2_h = nc.dram_tensor("c_sin2", [128, S], F32, kind="ExternalInput")
    mrot_h = nc.dram_tensor("c_mrot", [128, 128], F32, kind="ExternalInput")
    tri_h = nc.dram_tensor("c_tri", [128, 128], F32, kind="ExternalInput")
    idf_h = nc.dram_tensor("c_idf", [128, 128], F32, kind="ExternalInput")
    onesc_h = nc.dram_tensor("c_onesc", [128, 4], F32, kind="ExternalInput")
    onesr_h = nc.dram_tensor("c_onesr", [1, 128], F32, kind="ExternalInput")

    wq0 = nc.dram_tensor("wq0f", [DM, DM], F32, kind="ExternalInput")
    wk0 = nc.dram_tensor("wk0f", [DM, DM], F32, kind="ExternalInput")
    wv0 = nc.dram_tensor("wv0f", [DM, DM], F32, kind="ExternalInput")
    wo0 = nc.dram_tensor("L0_attn_Wo", [DM, DM], F32, kind="ExternalInput")
    wrt = nc.dram_tensor("wrt", [DM, DM], F32, kind="ExternalInput")
    wkt = nc.dram_tensor("wkt", [DM, DM], F32, kind="ExternalInput")
    wvt = nc.dram_tensor("wvt", [DM, DM], F32, kind="ExternalInput")
    mur_h = nc.dram_tensor("L0_ffn_mu_r", [DM], F32, kind="ExternalInput")
    muk_h = nc.dram_tensor("L0_ffn_mu_k", [DM], F32, kind="ExternalInput")
    wq1 = nc.dram_tensor("L1_attn_Wq", [DM, DM], F32, kind="ExternalInput")
    wk1 = nc.dram_tensor("L1_attn_Wk", [DM, DM], F32, kind="ExternalInput")
    wv1 = nc.dram_tensor("L1_attn_Wv", [DM, DM], F32, kind="ExternalInput")
    wo1 = nc.dram_tensor("L1_attn_Wo", [DM, DM], F32, kind="ExternalInput")
    bq1 = nc.dram_tensor("L1_attn_bq", [DM], F32, kind="ExternalInput")
    bk1 = nc.dram_tensor("L1_attn_bk", [DM], F32, kind="ExternalInput")
    bv1 = nc.dram_tensor("L1_attn_bv", [DM], F32, kind="ExternalInput")
    bo1 = nc.dram_tensor("L1_attn_bo", [DM], F32, kind="ExternalInput")
    wgate = nc.dram_tensor("L1_ffn_w_gate", [IFF, DM], F32,
                           kind="ExternalInput")
    wup = nc.dram_tensor("L1_ffn_w_up", [IFF, DM], F32, kind="ExternalInput")
    wdown = nc.dram_tensor("L1_ffn_w_down", [DM, IFF], F32,
                           kind="ExternalInput")
    wgt_c = nc.dram_tensor("wgt_c", [DM, IFF], BF16, kind="ExternalOutput")
    wut_c = nc.dram_tensor("wut_c", [DM, IFF], BF16, kind="ExternalOutput")
    wdt_c = nc.dram_tensor("wdt_c", [IFF, DM], BF16, kind="ExternalOutput")

    nphase = len(phases)
    nrp = 2 if nphase > 2 else max(1, nphase - 1)
    rp = [nc.dram_tensor(f"rscratch{i}", [NB, NCH, 128, S], F32,
                         kind="ExternalOutput") for i in range(nrp)]

    with ExitStack() as progctx:
        p = Prog(nc, progctx)
        cx = CX()
        cx._free = {}
        cx.free = lambda k: cx._free.get(k, ())
        cx.setfree = lambda k, *t: cx._free.__setitem__(k, tuple(t))
        cx.cos2_h = cos2_h.ap()
        cx.sin2_h = sin2_h.ap()

        sbuf = lambda n, shp, dt: progctx.enter_context(
            nc.sbuf_tensor(_nm(n), shp, dt))
        cx.sc1 = sbuf("g_sc1", [128, S], F32)
        cx.ra = sbuf("g_ra", [1, S], F32)
        cx.rb = sbuf("g_rb", [1, S], F32)
        cx.onesc = sbuf("g_onesc", [128, 4], BF16)
        cx.onesrf = sbuf("g_onesrf", [1, 128], F32)
        cx.onesrb = sbuf("g_onesrb", [1, 128], BF16)
        cx.mrot2 = sbuf("g_mrot2", [128, 128], BF16)
        cx.tri = sbuf("g_tri", [128, 128], F32)
        cx.idf = sbuf("g_idf", [128, 128], F32)
        cx.idb = sbuf("g_idb", [128, 128], BF16)
        cx.epsc = sbuf("g_epsc", [128, 1], F32)

        p.load("pool", cx.onesc[:], onesc_h.ap(), sem="ldm")
        p.load("sp", cx.onesrf[:], onesr_h.ap(), sem="ldm")
        p.load("pool", cx.onesrb[:], onesr_h.ap(), sem="ldm")
        p.load("pool", cx.mrot2[:], mrot_h.ap(), sem="ldm")
        p.load("sp", cx.tri[:], tri_h.ap(), sem="ldm")
        p.load("sp", cx.idf[:], idf_h.ap(), sem="ldm")
        p.load("pool", cx.idb[:], idf_h.ap(), sem="ldm")
        p.op("dve", "memset", cx.epsc[:], EPS_LN, inc="dve")
        const_tok = (p.tok("ldm"),)
        for e in ("pe", "dve", "act", "pool"):
            p.op(e, "nop", waits=const_tok)

        cx.wq_h = [wq0.ap()[:, pr * 128:(pr + 1) * 128]
                   .rearrange("(c p1) o -> p1 c o", p1=128)
                   for pr in range(H // 2)]
        cx.wk_h = [wk0.ap()[:, pr * 128:(pr + 1) * 128]
                   .rearrange("(c p1) o -> p1 c o", p1=128)
                   for pr in range(H // 2)]
        cx.wv_h = [wv0.ap()[:, vh * 512:(vh + 1) * 512]
                   .rearrange("(c p1) n -> p1 c n", p1=128)
                   for vh in range(2)]
        cx.wo_h = [wo0.ap()[:, oc * 128:(oc + 1) * 128]
                   .rearrange("(c p1) o -> p1 c o", p1=128)
                   for oc in range(NCH)]
        colblocks = lambda w: [w.ap()[:, oc * 128:(oc + 1) * 128]
                               .rearrange("(c p1) o -> p1 c o", p1=128)
                               for oc in range(NCH)]
        halves = lambda w: [w.ap()[:, vh * 512:(vh + 1) * 512]
                            .rearrange("(c p1) n -> p1 c n", p1=128)
                            for vh in range(2)]
        cx.wrt_h = colblocks(wrt)
        cx.wkt_h = colblocks(wkt)
        cx.wvt_h = colblocks(wvt)
        cx.mur_h = mur_h.ap()
        cx.muk_h = muk_h.ap()
        cx.wq1_h = colblocks(wq1)
        cx.wo1_h = colblocks(wo1)
        cx.wk1_h = halves(wk1)
        cx.wv1_h = halves(wv1)
        cx.bq_h = bq1.ap()
        cx.bk_h = bk1.ap()
        cx.bv_h = bv1.ap()
        cx.bo_h = bo1.ap()
        cx.wg_h = wgate.ap()
        cx.wu_h = wup.ap()
        cx.wd_h = wdown.ap()
        cx.wgt_h = wgt_c.ap()
        cx.wut_h = wut_c.ap()
        cx.wdt_h = wdt_c.ap()
        cx.wqc_tok = None

        emitters = {"att0": emit_att0, "lnonly": emit_lnonly,
                    "ffn0": emit_ffn0, "att1": emit_att1, "ffn1": emit_ffn1}
        cur = [[xin.ap()[b, c] for c in range(NCH)] for b in range(NB)]
        ready = {b: ("st", 0) for b in range(NB)}
        for i, ph in enumerate(phases):
            if i == nphase - 1:
                nxt = [[y.ap()[b, c] for c in range(NCH)] for b in range(NB)]
            else:
                rpt = rp[i % nrp]
                nxt = [[rpt.ap()[b, c] for c in range(NCH)] for b in range(NB)]
            for b in range(NB):
                ready[b] = emitters[ph](p, cx, b, cur[b], nxt[b], ready[b])
            cur = nxt

        finals = [(p.sem[s], p.cnt[s])
                  for s in ("st", "stp", "lds", "ldp", "ldm", "ldb")]
        def fin(e, finals=finals):
            for h, v in finals:
                if v > 0:
                    e.wait_ge(h, v)
        p.q["sp"].append(fin)

        with nc.Block() as block:
            p.finalize(block)

    return nc


def _prep_core_inputs(inputs, consts, core):
    xs = inputs["L0_attn_x"]
    bs = range(core * NB, (core + 1) * NB)
    xt = np.stack([np.ascontiguousarray(xs[b].T).reshape(NCH, 128, S)
                   for b in bs])
    m = {"xin": xt}
    m.update(consts)
    m["L0_attn_Wo"] = inputs["L0_attn_Wo"]
    for nm, k in (("wq0f", "L0_attn_Wq"), ("wk0f", "L0_attn_Wk"),
                  ("wv0f", "L0_attn_Wv")):
        m[nm] = np.ascontiguousarray(
            inputs[k].transpose(1, 0, 2).reshape(DM, DM))
    for nm, k in (("wrt", "L0_ffn_W_r"), ("wkt", "L0_ffn_W_k"),
                  ("wvt", "L0_ffn_W_v")):
        m[nm] = np.ascontiguousarray(inputs[k].T)
    for k in ("L0_ffn_mu_r", "L0_ffn_mu_k", "L1_attn_Wq", "L1_attn_bq",
              "L1_attn_Wk", "L1_attn_bk", "L1_attn_Wv", "L1_attn_bv",
              "L1_attn_Wo", "L1_attn_bo", "L1_ffn_w_gate", "L1_ffn_w_up",
              "L1_ffn_w_down"):
        m[k] = inputs[k]
    return m


def kernel(**inputs):
    inputs = {k: np.asarray(v) for k, v in inputs.items()}
    consts = host_consts()
    nc = build_program(_PHASES)
    in_maps = [_prep_core_inputs(inputs, consts, core)
               for core in range(NCORE)]
    res = run_bass_kernel_spmd(nc, in_maps, core_ids=list(range(NCORE)))
    out = np.empty((32, S, DM), np.float32)
    for core in range(NCORE):
        yv = np.asarray(res.results[core]["y"])
        for b in range(NB):
            out[core * NB + b] = yv[b].reshape(DM, S).T
    return out
